# version 01 (final)
# baseline (speedup 1.0000x reference)
import numpy as np
import concourse.bass as bass
import concourse.mybir as mybir
from concourse.bass_utils import run_bass_kernel_spmd
from concourse.aot_env import allow_ici_all_to_all
from contextlib import ExitStack

F32 = mybir.dt.float32
F16 = mybir.dt.float16
AF = mybir.ActivationFunctionType
ALU = mybir.AluOpType

import os
LIM = int(os.environ.get("MK_LIM", "5"))
SUB = int(os.environ.get("MK_SUB", "9"))
NC = 8
D = 2048
DC = 256
N = 16
CH = 2048
KEXP = 8
EPS = 1e-5


def build_nc(T):
    NCH = T // CH
    TSUB = CH // NC
    nc = bass.Bass("TRN2", target_bir_lowering=False)

    def din(name, shape, dt=F32):
        return nc.dram_tensor(name, shape, dt, kind="ExternalInput").ap()
    x_dcol = din("x_dcol", [T, DC])
    x_trow = din("x_trow", [T // NC, D])
    w_cw = din("w_cw", [DC, 4])
    w_cb = din("w_cb", [DC, 1])
    w_dtb = din("w_dtb", [DC, 1])
    w_alog = din("w_alog", [DC, N])
    w_dp = din("w_dp", [DC, 1])
    w_dtprojT = din("w_dtprojT", [128, DC])
    w_xprojT = din("w_xprojT", [DC, 160])
    w_w1 = din("w_w1", [D, D])
    w_b1 = din("w_b1", [D, 1])
    w_w2 = din("w_w2", [D, KEXP])
    w_b2 = din("w_b2", [KEXP, 1])
    w_vt = din("w_vt", [D, 512])
    w_ut = din("w_ut", [512, D])
    c_ident16 = din("c_ident16", [128, 128], F16)
    c_sel16 = din("c_sel16", [KEXP, 512], F16)
    c_zero32 = din("c_zero32", [128, 32])
    c_eps = din("c_eps", [128, 1])
    c_b2bc = din("c_b2bc", [128, 2 * KEXP])
    c_zero16 = din("c_zero16", [128, 16], F16)

    o_y = nc.dram_tensor("o_y", [T // NC, D], F32, kind="ExternalOutput").ap()

    w1f16 = nc.dram_tensor("w1f16", [16 * 16 * 128, 128], F16)
    ar1_in = [nc.dram_tensor(f"ar1i{c}", [128, 32], F32) for c in range(NCH)]
    ar1_out = [nc.dram_tensor(f"ar1o{c}", [128, 32], F32) for c in range(NCH)]
    ar2_in = [nc.dram_tensor(f"ar2i{c}", [160, CH], F16) for c in range(NCH)]
    ar2_out = [nc.dram_tensor(f"ar2o{c}", [160, CH], F16) for c in range(NCH)]
    a2a_in = [nc.dram_tensor(f"a2ai{c}", [CH, DC], F16) for c in range(NCH)]
    a2a_out = [nc.dram_tensor(f"a2ao{c}", [CH, DC], F16) for c in range(NCH)]

    with ExitStack() as st:
        def sb(name, shape, dt):
            return st.enter_context(nc.sbuf_tensor(name, shape, dt))
        def psum(name, shape, dt):
            return st.enter_context(nc.psum_tensor(name, shape, dt))
        def sem(name):
            return st.enter_context(nc.semaphore(name))

        s_xprojT = sb("s_xprojT", [128, 2 * 160], F16)
        s_dtprojT = sb("s_dtprojT", [128, DC], F16)
        s_vt = sb("s_vt", [128, 16 * 512], F16)
        s_ut = sb("s_ut", [128, 4 * D], F16)
        s_w2 = sb("s_w2", [128, 16 * KEXP], F16)
        s_cw = sb("s_cw", [128, 2 * 4], F32)
        s_cb = sb("s_cb", [128, 2], F32)
        s_dtb = sb("s_dtb", [128, 2], F32)
        s_dp = sb("s_dp", [128, 2], F32)
        s_Alog = sb("s_Alog", [128, 2 * N], F32)
        s_A = sb("s_A", [128, 2 * N], F32)
        s_b1 = sb("s_b1", [128, 16], F32)
        s_b2 = sb("s_b2", [KEXP, 1], F32)
        s_ident = sb("s_ident", [128, 128], F16)
        s_eps = sb("s_eps", [128, 1], F32)
        s_b2bc = sb("s_b2bc", [128, 2 * KEXP], F32)
        s_sel = sb("s_sel", [KEXP, 512], F16)
        s_xpro = sb("s_xpro", [128, 16 * 256], F32)
        s_stats = sb("s_stats", [128, 32], F32)
        s_statf = sb("s_statf", [128, 48], F32)
        s_x16 = sb("s_x16", [128, 2 * 4 * 256], F16)
        s_ln1x = sb("s_ln1x", [128, 2 * (4 + CH)], F32)
        s_xconv = sb("s_xconv", [128, 2 * CH], F16)
        s_proj16 = sb("s_proj16", [128, CH], F16)
        s_proj16b = sb("s_proj16b", [32, CH], F16)
        s_delta = sb("s_delta", [128, 2 * CH], F16)
        s_deltax = sb("s_deltax", [128, 2 * CH], F16)
        s_a = sb("s_a", [128, 2 * CH], F16)
        s_bb = sb("s_bb", [128, 2 * CH], F16)
        s_Bbc = sb("s_Bbc", [128, 2 * CH], F16)
        s_Cbc = sb("s_Cbc", [128, 2 * CH], F16)
        s_h = sb("s_h", [128, 2 * 2 * CH], F16)
        s_carry = sb("s_carry", [128, 2 * N], F32)
        s_ypart = sb("s_ypart", [128, 2 * CH], F16)
        s_zfm = sb("s_zfm", [128, 16 * TSUB], F16)
        s_yep = sb("s_yep", [128, 2 * D], F16)
        s_xp = sb("s_xp", [128, 2 * D], F16)
        s_z = sb("s_z", [128, D], F16)
        s_epst = sb("s_epst", [128, 16], F32)
        s_epf = sb("s_epf", [128, 16], F32)
        s_w1s = sb("s_w1s", [128, 16 * 128], F16)
        s_he = sb("s_he", [128, TSUB], F16)
        s_alpha = sb("s_alpha", [KEXP, TSUB], F16)
        s_atm = sb("s_atm", [128, 2 * KEXP], F32)
        s_aexp = sb("s_aexp", [128, 2 * KEXP], F32)
        s_asm = sb("s_asm", [128, 8], F32)
        s_anrm = sb("s_anrm", [128, 2 * KEXP], F16)
        s_af = sb("s_af", [KEXP, TSUB], F16)
        s_vx = sb("s_vx", [128, 4 * TSUB], F16)
        s_va = sb("s_va", [128, 4 * TSUB], F16)
        s_ffsb = sb("s_ffsb", [128, 4 * 2 * 128], F16)
        s_fftm = sb("s_fftm", [128, 2 * D], F16)

        s_wld = s_xpro[:, 0:2048]
        s_out = s_xpro[:, 0:2048]
        s_wld16 = s_delta[:, 0:2048]
        s_xep = s_xpro[:, 2048:4096]
        s_dt16 = s_w1s[:, 0:2048]
        s_spln = s_bb[:]
        s_ytmo = s_zfm[:]
        junkv = s_Cbc[:, 0:2048]
        junka = s_a[:, 0:2048]
        s_u = s_z[:]

        pt0 = psum("pt0", [128, 512], F32)
        pt1 = psum("pt1", [128, 512], F32)
        pp0 = psum("pp0", [128, 512], F32)
        pj = psum("pj", [128, 512], F32)
        pd0 = psum("pd0", [128, 512], F32)
        pd1 = psum("pd1", [128, 512], F32)
        pf0 = psum("pf0", [128, 512], F32)
        pf1 = psum("pf1", [128, 512], F32)
        pds = [pd0, pd1]
        pfs = [pf0, pf1]
        pts = [pt0, pt1]

        m_ld = sem("m_ld"); m_x1 = sem("m_x1"); m_sti = sem("m_sti")
        m_srd = sem("m_srd"); m_p16 = sem("m_p16"); m_dtr = sem("m_dtr")
        m_bc = sem("m_bc"); m_w1 = sem("m_w1"); m_yo = sem("m_yo")
        m_ye = sem("m_ye"); m_xe = sem("m_xe"); m_out = sem("m_out")
        m_cc = sem("m_cc")
        m_wc = sem("m_wc"); m_w1s = sem("m_w1s")
        m_bB0 = sem("m_bB0"); m_bB1 = sem("m_bB1")
        m_bC0 = sem("m_bC0"); m_bC1 = sem("m_bC1")
        m_bBs = [m_bB0, m_bB1]; m_bCs = [m_bC0, m_bC1]
        v1 = sem("v1"); v1b = sem("v1b"); v2a = sem("v2a"); v2 = sem("v2")
        v3 = sem("v3"); v4 = sem("v4"); v5 = sem("v5"); v6 = sem("v6")
        vq = sem("vq"); v7 = sem("v7"); v8 = sem("v8"); v9 = sem("v9")
        v10 = sem("v10"); v11 = sem("v11"); v12 = sem("v12"); ve1 = sem("ve1")
        ve2 = sem("ve2"); vsu = sem("vsu"); vrc = sem("vrc")
        a1 = sem("a1"); a1b = sem("a1b"); a2 = sem("a2"); a3 = sem("a3")
        a4 = sem("a4"); a5 = sem("a5"); a6 = sem("a6"); a7 = sem("a7")
        a8 = sem("a8"); a10 = sem("a10"); a11 = sem("a11"); a12 = sem("a12")
        a13 = sem("a13"); a14 = sem("a14")
        tq = sem("tq"); t2 = sem("t2"); t3 = sem("t3"); t4 = sem("t4")
        t5 = sem("t5"); t6 = sem("t6"); t7 = sem("t7"); t8 = sem("t8")

        blk = st.enter_context(nc.Block())

        NLOAD = 16
        CCPC = 1 if LIM < 2 else (2 if LIM < 3 else 3)
        YQ = 16
        TQ_CH = 34 if LIM >= 5 else (24 if LIM == 4 else (16 if LIM == 3 else 8))

        def wldv(cols, inner):
            return s_wld[:, 0:cols].rearrange("p (a b) -> p a b", b=inner)
        SETUP_CASTS = [
            (w_xprojT.rearrange("(dt p) f -> p dt f", p=128),
             wldv(320, 160), s_xprojT[:], 320),
            (w_dtprojT, s_wld[:, 0:256], s_dtprojT[:], 256),
            (w_w2.rearrange("(k p) f -> p k f", p=128),
             wldv(128, KEXP), s_w2[:], 128),
        ] + [
            (w_vt.rearrange("(k p) f -> p k f", p=128)[:, k2:k2 + 1, :],
             wldv(512, 512), s_vt[:, 512 * k2:512 * (k2 + 1)], 512)
            for k2 in range(16)
        ] + [
            (w_ut.rearrange("(i p) f -> p i f", p=128)[:, i2:i2 + 1, :],
             wldv(2048, 2048), s_ut[:, D * i2:D * (i2 + 1)], 2048)
            for i2 in range(4)
        ]
        NCAST = len(SETUP_CASTS)

        @blk.sync
        def _(sync):
            def dma(out, in_, slow=False):
                sync.dma_start(out=out, in_=in_,
                               allow_slow_non_contiguous=slow).then_inc(m_ld, 16)
            dma(s_cw[:].rearrange("p (dt f) -> p dt f", dt=2),
                w_cw.rearrange("(dt p) f -> p dt f", p=128))
            dma(s_cb[:].rearrange("p (dt o) -> p dt o", o=1),
                w_cb.rearrange("(dt p) f -> p dt f", p=128), slow=True)
            dma(s_dtb[:].rearrange("p (dt o) -> p dt o", o=1),
                w_dtb.rearrange("(dt p) f -> p dt f", p=128), slow=True)
            dma(s_dp[:].rearrange("p (dt o) -> p dt o", o=1),
                w_dp.rearrange("(dt p) f -> p dt f", p=128), slow=True)
            dma(s_Alog[:].rearrange("p (dt f) -> p dt f", dt=2),
                w_alog.rearrange("(dt p) f -> p dt f", p=128))
            dma(s_b1[:].rearrange("p (e o) -> p e o", o=1),
                w_b1.rearrange("(e p) f -> p e f", p=128), slow=True)
            dma(s_b2[:], w_b2)
            dma(s_ident[:], c_ident16)
            dma(s_sel[:], c_sel16)
            dma(s_carry[:], c_zero32)
            dma(s_ln1x[:].rearrange("p (dt f) -> p dt f", dt=2)[:, :, 0:4],
                c_zero32[:, 0:8].rearrange("p (dt f) -> p dt f", dt=2))
            dma(s_epst[:], c_zero32[:, 0:16])
            dma(s_stats[:], c_zero32)
            dma(s_statf[:, 0:32], c_zero32)
            dma(s_eps[:], c_eps)
            dma(s_b2bc[:], c_b2bc)
            for i2, (src_ap, ld_ap, dst_ap, cols) in enumerate(SETUP_CASTS):
                sync.wait_ge(vsu, i2)
                sync.dma_start(out=ld_ap, in_=src_ap).then_inc(m_wc, 16)
            for k in range(16):
                sync.wait_ge(vsu, NCAST + k)
                sync.dma_start(out=s_wld, in_=w_w1[128 * k:128 * (k + 1), :]
                               ).then_inc(m_wc, 16)
                sync.wait_ge(vsu, NCAST + k + 1)
                sync.dma_start(
                    out=w1f16.ap().rearrange("(e k2 p) c -> e k2 p c", k2=16, p=128
                                             )[:, k, :, :].transpose([1, 0, 2]),
                    in_=s_wld16.rearrange("p (e c) -> p e c", e=16),
                ).then_inc(m_w1s, 16)
            for c in range(NCH):
                if c >= 1 and LIM >= 5:
                    sync.wait_ge(m_out, 32 * c)
                sync.dma_start(
                    out=s_xpro[:].rearrange("p (i d2) -> p i d2", i=16),
                    in_=x_dcol[CH * c:CH * (c + 1), :]
                        .rearrange("(i p) d2 -> i p d2", p=128).transpose([1, 0, 2]),
                ).then_inc(m_x1, 16)
                sync.wait_ge(v1, c + 1)
                sync.wait_ge(a1, c + 1)
                sync.dma_start(out=ar1_in[c].ap(), in_=s_stats[:]).then_inc(m_sti, 16)
                sync.wait_ge(m_cc, CCPC * c + 1)
                sync.dma_start(out=s_stats[:], in_=ar1_out[c].ap()).then_inc(m_srd, 16)
                if LIM < 2 or SUB < 3:
                    continue
                sync.wait_ge(a7, 4 * (c + 1))
                sync.dma_start(out=ar2_in[c].ap()[0:128, :], in_=s_proj16[:]
                               ).then_inc(m_p16, 16)
                sync.dma_start(out=ar2_in[c].ap()[128:160, :], in_=s_proj16b[:]
                               ).then_inc(m_p16, 16)
                if SUB < 5:
                    continue
                sync.wait_ge(m_cc, CCPC * c + 2)
                if LIM >= 5:
                    sync.wait_ge(t4, 16 * c)
                sync.dma_start(out=s_dt16, in_=ar2_out[c].ap()[0:128, :]
                               ).then_inc(m_dtr, 16)
                if LIM < 3:
                    continue
                sync.wait_ge(vq, TQ_CH * c + YQ)
                sync.dma_start(
                    out=a2a_in[c].ap().rearrange("(i p) d2 -> i p d2", p=128
                                                 ).transpose([1, 0, 2]),
                    in_=s_ytmo.rearrange("p (i d2) -> p i d2", i=16),
                ).then_inc(m_yo, 16)
                sync.wait_ge(m_cc, CCPC * c + 3)
                if LIM < 4:
                    continue
                for tl in range(2):
                    sync.dma_start(
                        out=s_yep[:, D * tl:D * (tl + 1)]
                            .rearrange("p (r d2) -> p r d2", r=NC),
                        in_=a2a_out[c].ap()
                            .rearrange("(r tl p) d2 -> r tl p d2", tl=2, p=128)
                            [:, tl, :, :].transpose([1, 0, 2]),
                    ).then_inc(m_ye, 16)
                for tl in range(2):
                    if tl == 1:
                        sync.wait_ge(ve1, 2 * c + 1)
                    sync.dma_start(
                        out=s_xep,
                        in_=x_trow[TSUB * c + 128 * tl:TSUB * c + 128 * (tl + 1), :],
                    ).then_inc(m_xe, 16)
                if LIM < 5:
                    continue
                sync.wait_ge(t3, 8 * (c + 1))
                if c == 0:
                    sync.wait_ge(m_w1s, 256)
                for e in range(16):
                    sync.wait_ge(t4, 16 * c + e)
                    sync.dma_start(
                        out=s_w1s[:].rearrange("p (k2 c2) -> p k2 c2", k2=16),
                        in_=w1f16.ap().rearrange("(e2 k2 p) c2 -> e2 k2 p c2",
                                                 k2=16, p=128)[e, :, :, :]
                            .transpose([1, 0, 2]),
                    ).then_inc(m_w1, 16)
                for tl in range(2):
                    sync.wait_ge(v11, 2 * c + tl + 1)
                    sync.dma_start(
                        out=o_y[TSUB * c + 128 * tl:TSUB * c + 128 * (tl + 1), :],
                        in_=s_out,
                    ).then_inc(m_out, 16)
            if LIM >= 5:
                sync.wait_ge(m_out, 32 * NCH)
            elif LIM == 4:
                sync.wait_ge(ve2, 2 * NCH)
            elif LIM == 3:
                sync.wait_ge(m_ye, 0)
                sync.wait_ge(m_cc, CCPC * NCH)
            elif LIM == 2:
                if SUB >= 6:
                    sync.wait_ge(v3, NCH)
                elif SUB == 5:
                    sync.wait_ge(a3, 8 * NCH)
                elif SUB == 4:
                    sync.wait_ge(m_cc, CCPC * NCH)
                elif SUB == 3:
                    sync.wait_ge(a7, 4 * NCH)
                elif SUB == 2:
                    sync.wait_ge(a2, NCH)
                else:
                    sync.wait_ge(v2, NCH)
            else:
                sync.wait_ge(m_srd, 16 * NCH)
                sync.wait_ge(v1b, 1 + 7 * (NCH - 1))

        @blk.gpsimd
        def _(gpsimd):
            rg = [list(range(NC))]
            for c in range(NCH):
                gpsimd.wait_ge(m_sti, 16 * (c + 1))
                gpsimd.collective_compute(
                    "AllReduce", ALU.add, replica_groups=rg,
                    ins=[ar1_in[c].ap().opt()], outs=[ar1_out[c].ap().opt()],
                ).then_inc(m_cc, 1)
                if LIM < 2 or SUB < 4:
                    continue
                gpsimd.wait_ge(m_p16, 32 * (c + 1))
                gpsimd.collective_compute(
                    "AllReduce", ALU.add, replica_groups=rg,
                    ins=[ar2_in[c].ap().opt()], outs=[ar2_out[c].ap().opt()],
                ).then_inc(m_cc, 1)
                if LIM < 3:
                    continue
                gpsimd.wait_ge(m_yo, 16 * (c + 1))
                with allow_ici_all_to_all():
                    gpsimd.collective_compute(
                        "AllToAll", ALU.bypass, replica_groups=rg,
                        ins=[a2a_in[c].ap().opt()], outs=[a2a_out[c].ap().opt()],
                    ).then_inc(m_cc, 1)

        @blk.tensor
        def _(tensor):
            tensor.wait_ge(m_ld, 16 * NLOAD)
            tq_n = [0]

            def chaser():
                tensor.matmul(pj[:, 0:512], s_ident[:], s_vt[:, 0:512],
                              start=True, stop=True)
                tensor.nop()

            def tr_group(subtiles, wait_sems, first=False):
                g = tq_n[0]
                bank = pts[g % 2]
                for w_sem, w_val in wait_sems:
                    tensor.wait_ge(w_sem, w_val)
                if g >= 2:
                    tensor.wait_ge(vq, g - 1)
                off = 0
                for in_ap in subtiles:
                    pr, fr = in_ap.shape[0], in_ap.shape[1]
                    tensor.matmul(bank[0:fr, off:off + pr], in_ap,
                                  s_ident[0:pr, 0:pr], start=True, stop=True)
                    tensor.nop()
                    off += pr
                if not first:
                    tensor.drain().then_inc(tq, 1)
                tq_n[0] += 1

            def tr_flush():
                chaser()
                tensor.drain().then_inc(tq, 1)

            for c in range(NCH):
                if LIM < 2:
                    continue
                if c == 0:
                    tensor.wait_ge(vsu, NCAST)
                for b in range(4):
                    for h in range(2):
                        buf = s_x16[:, 1024 * (b % 2):1024 * (b % 2) + 1024]
                        subs = [buf[:, 256 * i + 128 * h:256 * i + 128 * h + 128]
                                for i in range(4)]
                        tr_group(subs, [(v2a, 4 * c + b + 1)],
                                 first=(b == 0 and h == 0))
                tr_flush()
                if SUB < 3:
                    continue
                tensor.wait_ge(a2, c + 1)
                pps = [pp0, pd0]
                for p in range(4):
                    if 4 * c + p >= 2:
                        tensor.wait_ge(a7, 4 * c + p - 1)
                    for dt in range(2):
                        tensor.matmul(pps[p % 2][:, :],
                                      s_xprojT[:, 160 * dt:160 * dt + 128],
                                      s_xconv[:, CH * dt + 512 * p:
                                              CH * dt + 512 * (p + 1)],
                                      start=(dt == 0), stop=(dt == 1))
                        tensor.nop()
                        if dt == 0 and p >= 1:
                            tensor.drain().then_inc(t2, 1)
                    for dt in range(2):
                        tensor.matmul(pd1[32 * (p % 2):32 * (p % 2) + 32, :],
                                      s_xprojT[:, 160 * dt + 128:160 * (dt + 1)],
                                      s_xconv[:, CH * dt + 512 * p:
                                              CH * dt + 512 * (p + 1)],
                                      start=(dt == 0), stop=(dt == 1))
                        tensor.nop()
                chaser()
                tensor.drain().then_inc(t2, 1)
                if SUB < 5:
                    continue
                tensor.wait_ge(m_dtr, 16 * (c + 1))
                for idx in range(8):
                    dt, cp = idx // 4, idx % 4
                    if 8 * c + idx >= 2:
                        tensor.wait_ge(a3, 8 * c + idx - 1)
                    tensor.matmul(pds[idx % 2][:, :],
                                  s_dtprojT[:, 128 * dt:128 * (dt + 1)],
                                  s_dt16[:, 512 * cp:512 * (cp + 1)],
                                  start=True, stop=True)
                    tensor.nop()
                    if idx >= 1:
                        tensor.drain().then_inc(t3, 1)
                chaser()
                tensor.drain().then_inc(t3, 1)
                if LIM < 3:
                    continue
                for g in range(8):
                    subs = []
                    for i in range(2):
                        ti = 2 * g + i
                        for dt in range(2):
                            subs.append(s_ypart[:, CH * dt + 128 * ti:
                                                CH * dt + 128 * (ti + 1)])
                    tr_group(subs, [(v6, c + 1)], first=(g == 0))
                tr_flush()
                if LIM < 4:
                    continue
                for tl in range(2):
                    for gg in range(4):
                        subs = [s_z[:, 128 * (4 * gg + i):128 * (4 * gg + i + 1)]
                                for i in range(4)]
                        tr_group(subs, [(ve2, 2 * c + tl + 1)], first=(gg == 0))
                    tr_flush()
                if LIM < 5:
                    continue
                for e in range(16):
                    tensor.wait_ge(m_w1, 256 * c + 16 * (e + 1))
                    if e == 0:
                        tensor.wait_ge(v7, c + 1)
                        tensor.wait_ge(v9, 4 * c)
                        tensor.wait_ge(v10, 16 * c)
                    for k in range(16):
                        tensor.matmul(pf0[:, 0:TSUB],
                                      s_w1s[:, 128 * k:128 * (k + 1)],
                                      s_zfm[:, TSUB * k:TSUB * (k + 1)],
                                      start=(k == 0), stop=(k == 15))
                        tensor.nop()
                    chaser()
                    tensor.drain().then_inc(t4, 1)
                    tensor.wait_ge(a6, 16 * c + e + 1)
                    tensor.matmul(pf1[0:KEXP, 0:TSUB],
                                  s_w2[:, KEXP * e:KEXP * (e + 1)],
                                  s_he[:, :], start=(e == 0), stop=(e == 15))
                    tensor.nop()
                chaser()
                tensor.drain().then_inc(t5, 1)
                tr_group([s_alpha[:, 0:128], s_alpha[:, 128:256]],
                         [(a8, c + 1)], first=True)
                tr_flush()
                for i in range(4):
                    if i >= 2:
                        tensor.wait_ge(v8, 4 * c + i - 1)
                    for k in range(16):
                        tensor.matmul(pfs[i % 2][:, 0:TSUB],
                                      s_vt[:, 512 * k + 128 * i:
                                           512 * k + 128 * (i + 1)],
                                      s_zfm[:, TSUB * k:TSUB * (k + 1)],
                                      start=(k == 0), stop=(k == 15))
                        tensor.nop()
                        if k == 1 and i >= 1:
                            tensor.drain().then_inc(t6, 1)
                chaser()
                tensor.drain().then_inc(t6, 1)
                tr_group([s_anrm[:, 0:KEXP], s_anrm[:, KEXP:2 * KEXP]],
                         [(v12, 4 * c + 3)], first=True)
                tr_flush()
                tensor.wait_ge(a13, c + 1)
                tensor.wait_ge(v8, 4 * (c + 1))
                for i in range(4):
                    if i > 0:
                        tensor.wait_ge(v9, 4 * c + i)
                    tensor.matmul(pf0[:, 256 * (i % 2):256 * (i % 2) + TSUB],
                                  s_sel[:, 128 * i:128 * (i + 1)],
                                  s_af[:, :], start=True, stop=True)
                    tensor.nop()
                    chaser()
                    tensor.drain().then_inc(t7, 1)
                tensor.wait_ge(v9, 4 * (c + 1))
                for gg in range(9):
                    if gg < 8:
                        for dl in range(2):
                            dto = 2 * gg + dl
                            if dto >= 2:
                                tensor.wait_ge(v10, 16 * c + dto - 1)
                            for i in range(4):
                                tensor.matmul(pfs[dto % 2][:, 0:TSUB],
                                              s_ut[:, D * i + 128 * dto:
                                                   D * i + 128 * (dto + 1)],
                                              s_va[:, TSUB * i:TSUB * (i + 1)],
                                              start=(i == 0), stop=(i == 3))
                                tensor.nop()
                            if dto >= 1:
                                tensor.drain().then_inc(t8, 1)
                        if gg == 7:
                            chaser()
                            tensor.drain().then_inc(t8, 1)
                    if gg >= 1:
                        gp = gg - 1
                        subs = []
                        for tl in range(2):
                            for dl in range(2):
                                dto = 2 * gp + dl
                                subs.append(s_ffsb[:, 256 * (dto % 4) + 128 * tl:
                                                   256 * (dto % 4) + 128 * (tl + 1)])
                        tr_group(subs, [(v10, 16 * c + 2 * (gp + 1))],
                                 first=(gp == 0))
                tr_flush()

        @blk.vector
        def _(vector):
            vector.wait_ge(m_ld, 16 * NLOAD)
            for i2, (src_ap, ld_ap, dst_ap, cols) in enumerate(SETUP_CASTS):
                vector.wait_ge(m_wc, 16 * (i2 + 1))
                vector.tensor_copy(out=dst_ap, in_=s_wld[:, 0:cols])
                vector.drain().then_inc(vsu, 1)
            for k in range(16):
                vector.wait_ge(m_wc, 16 * (NCAST + k + 1))
                if k > 0:
                    vector.wait_ge(m_w1s, 16 * k)
                vector.tensor_copy(out=s_wld16, in_=s_wld)
                vector.drain().then_inc(vsu, 1)
            vector.wait_ge(a14, 1)
            vector.tensor_scalar(out=s_A[:], in0=s_A[:], scalar1=-1.0,
                                 scalar2=None, op0=ALU.mult)
            vector.drain().then_inc(vrc, 1)

            vq_n = [0]

            def evac(dst_list):
                g = vq_n[0]
                bank = pts[g % 2]
                vector.wait_ge(tq, g + 1)
                off = 0
                for dst, cols in dst_list:
                    src_ap = bank[:, off:off + cols]
                    if len(dst.shape) == 3:
                        src_ap = src_ap.rearrange("p (a b) -> p a b",
                                                  a=dst.shape[1])
                    vector.tensor_copy(out=dst, in_=src_ap)
                    off += cols
                vector.drain().then_inc(vq, 1)
                vq_n[0] += 1

            def norm_finish(sum_col, out_base, sq_wait, sq_val):
                vector.tensor_scalar(out=s_epf[:, out_base:out_base + 2],
                                     in0=s_epst[:, sum_col:sum_col + 2],
                                     scalar1=1.0 / D, scalar2=None, op0=ALU.mult)
                vector.drain()
                vector.tensor_tensor(out=s_epf[:, out_base + 2:out_base + 3],
                                     in0=s_epf[:, out_base:out_base + 1],
                                     in1=s_epf[:, out_base:out_base + 1],
                                     op=ALU.mult)
                vector.drain()
                vector.tensor_tensor(out=s_epf[:, out_base + 1:out_base + 2],
                                     in0=s_epf[:, out_base + 1:out_base + 2],
                                     in1=s_epf[:, out_base + 2:out_base + 3],
                                     op=ALU.subtract)
                vector.drain().then_inc(v1b, 1)
                vector.wait_ge(sq_wait, sq_val)
                vector.reciprocal(out=s_epf[:, out_base + 1:out_base + 2],
                                  in_=s_epf[:, out_base + 1:out_base + 2])
                vector.drain()
                vector.tensor_tensor(out=s_epf[:, out_base + 2:out_base + 3],
                                     in0=s_epf[:, out_base:out_base + 1],
                                     in1=s_epf[:, out_base + 1:out_base + 2],
                                     op=ALU.mult)
                vector.drain()

            for c in range(NCH):
                vector.wait_ge(m_x1, 16 * (c + 1))
                for i in range(16):
                    vector.tensor_scalar(
                        out=junkv[:, 0:256],
                        in0=s_xpro[:, 256 * i:256 * (i + 1)],
                        scalar1=1.0, scalar2=0.0, op0=ALU.mult, op1=ALU.add,
                        accum_out=s_stats[:, 2 * i:2 * i + 1])
                vector.drain().then_inc(v1, 1)
                vector.wait_ge(m_srd, 16 * (c + 1))
                vector.tensor_scalar(
                    out=s_statf[:, 0:16],
                    in0=s_stats[:].rearrange("p (i two) -> p i two", two=2
                                             )[:, :, 0:1].rearrange("p i o -> p (i o)"),
                    scalar1=1.0 / D, scalar2=None, op0=ALU.mult)
                vector.tensor_scalar(
                    out=s_statf[:, 32:48],
                    in0=s_stats[:].rearrange("p (i two) -> p i two", two=2
                                             )[:, :, 1:2].rearrange("p i o -> p (i o)"),
                    scalar1=1.0 / D, scalar2=None, op0=ALU.mult)
                vector.drain()
                vector.tensor_tensor(out=s_statf[:, 16:32], in0=s_statf[:, 0:16],
                                     in1=s_statf[:, 0:16], op=ALU.mult)
                vector.drain()
                vector.tensor_tensor(out=s_statf[:, 32:48], in0=s_statf[:, 32:48],
                                     in1=s_statf[:, 16:32], op=ALU.subtract)
                vector.drain().then_inc(v1b, 1)
                vector.wait_ge(a1b, c + 1)
                vector.reciprocal(out=s_statf[:, 16:32], in_=s_statf[:, 16:32])
                vector.drain()
                vector.tensor_tensor(out=s_statf[:, 32:48], in0=s_statf[:, 0:16],
                                     in1=s_statf[:, 16:32], op=ALU.mult)
                vector.drain()
                if LIM < 2:
                    continue

                def apply(b):
                    if 4 * c + b >= 4:
                        if b >= 2:
                            vector.wait_ge(tq, TQ_CH * c + 2 * b - 2)
                        else:
                            vector.wait_ge(tq, TQ_CH * (c - 1) + 2 * (b + 2) + 2)
                    buf = s_x16[:, 1024 * (b % 2):1024 * (b % 2) + 1024]
                    for j in range(4):
                        i = 4 * b + j
                        vector.tensor_scalar(
                            out=buf[:, 256 * j:256 * (j + 1)],
                            in0=s_xpro[:, 256 * i:256 * (i + 1)],
                            scalar1=s_statf[:, 16 + i:17 + i],
                            scalar2=s_statf[:, 32 + i:33 + i],
                            op0=ALU.mult, op1=ALU.subtract)
                    vector.drain().then_inc(v2a, 1)

                apply(0)
                apply(1)
                for g4 in range(4):
                    evac([(s_ln1x[:, 4 + 512 * g4:4 + 512 * (g4 + 1)], 512)])
                    evac([(s_ln1x[:, (4 + CH) + 4 + 512 * g4:
                                  (4 + CH) + 4 + 512 * (g4 + 1)], 512)])
                    if g4 + 2 < 4:
                        apply(g4 + 2)
                for dt in range(2):
                    base = (4 + CH) * dt
                    vector.tensor_scalar(
                        out=s_xconv[:, CH * dt:CH * (dt + 1)],
                        in0=s_ln1x[:, base + 1:base + 1 + CH],
                        scalar1=s_cw[:, 4 * dt:4 * dt + 1], scalar2=None,
                        op0=ALU.mult)
                    for j in range(1, 4):
                        vector.scalar_tensor_tensor(
                            out=s_xconv[:, CH * dt:CH * (dt + 1)],
                            in0=s_ln1x[:, base + 1 + j:base + 1 + j + CH],
                            scalar=s_cw[:, 4 * dt + j:4 * dt + j + 1],
                            in1=s_xconv[:, CH * dt:CH * (dt + 1)],
                            op0=ALU.mult, op1=ALU.add)
                    vector.tensor_copy(out=s_ln1x[:, base + 1:base + 4],
                                       in_=s_ln1x[:, base + 1 + CH:base + 4 + CH])
                vector.drain().then_inc(v2, 1)
                if SUB < 6:
                    continue
                vector.wait_ge(a3, 8 * (c + 1))
                for dt in range(2):
                    vector.tensor_tensor(out=s_delta[:, CH * dt:CH * (dt + 1)],
                                         in0=s_delta[:, CH * dt:CH * (dt + 1)],
                                         in1=s_spln[:, CH * dt:CH * (dt + 1)],
                                         op=ALU.add)
                    vector.tensor_tensor(out=s_deltax[:, CH * dt:CH * (dt + 1)],
                                         in0=s_delta[:, CH * dt:CH * (dt + 1)],
                                         in1=s_xconv[:, CH * dt:CH * (dt + 1)],
                                         op=ALU.mult)
                for dt in range(2):
                    base = (4 + CH) * dt
                    vector.tensor_scalar(out=s_ypart[:, CH * dt:CH * (dt + 1)],
                                         in0=s_ln1x[:, base + 4:base + 4 + CH],
                                         scalar1=s_dp[:, dt:dt + 1], scalar2=None,
                                         op0=ALU.mult)
                vector.drain().then_inc(v3, 1)
                if LIM < 3:
                    continue
                for n in range(N):
                    for dt in range(2):
                        s = 2 * n + dt
                        par = s % 2
                        vector.wait_ge(m_bBs[n % 2],
                                       16 * (8 * c + n // 2 + 1))
                        vector.tensor_tensor(
                            out=s_bb[:, CH * par:CH * (par + 1)],
                            in0=s_deltax[:, CH * dt:CH * (dt + 1)],
                            in1=s_Bbc[:, CH * (n % 2):CH * (n % 2 + 1)],
                            op=ALU.mult)
                        vector.wait_ge(a4, 32 * c + s + 1)
                        vector.tensor_tensor_scan(
                            out=s_h[:, CH * (2 * (n % 2) + dt):
                                    CH * (2 * (n % 2) + dt) + CH],
                            data0=s_a[:, CH * par:CH * (par + 1)],
                            data1=s_bb[:, CH * par:CH * (par + 1)],
                            initial=s_carry[:, N * dt + n:N * dt + n + 1],
                            op0=ALU.mult, op1=ALU.add)
                        vector.drain().then_inc(v4, 1)
                    if n >= 1:
                        pn = n - 1
                        vector.wait_ge(a5, 16 * c + n)
                        vector.wait_ge(m_bCs[pn % 2],
                                       16 * (8 * c + pn // 2 + 1))
                        for dt in range(2):
                            hsl = s_h[:, CH * (2 * (pn % 2) + dt):
                                      CH * (2 * (pn % 2) + dt) + CH]
                            vector.tensor_tensor(
                                out=hsl, in0=hsl,
                                in1=s_Cbc[:, CH * (pn % 2):CH * (pn % 2 + 1)],
                                op=ALU.mult)
                            vector.tensor_tensor(
                                out=s_ypart[:, CH * dt:CH * (dt + 1)],
                                in0=s_ypart[:, CH * dt:CH * (dt + 1)],
                                in1=hsl, op=ALU.add)
                        vector.drain().then_inc(v5, 1)
                pn = N - 1
                vector.wait_ge(a5, 16 * c + 16)
                vector.wait_ge(m_bCs[pn % 2], 16 * (8 * c + pn // 2 + 1))
                for dt in range(2):
                    hsl = s_h[:, CH * (2 * (pn % 2) + dt):CH * (2 * (pn % 2) + dt) + CH]
                    vector.tensor_tensor(out=hsl, in0=hsl,
                                         in1=s_Cbc[:, CH * (pn % 2):CH * (pn % 2 + 1)],
                                         op=ALU.mult)
                    vector.tensor_tensor(out=s_ypart[:, CH * dt:CH * (dt + 1)],
                                         in0=s_ypart[:, CH * dt:CH * (dt + 1)],
                                         in1=hsl, op=ALU.add)
                vector.drain().then_inc(v5, 1)
                vector.sem_inc(v6, 1)
                for g in range(8):
                    evac([(s_ytmo[:, 512 * g:512 * (g + 1)], 512)])
                if LIM < 4:
                    continue
                vector.wait_ge(m_ye, 32 * (c + 1))
                for tl in range(2):
                    vector.tensor_scalar(
                        out=junkv[:, 0:D], in0=s_yep[:, D * tl:D * (tl + 1)],
                        scalar1=1.0, scalar2=0.0, op0=ALU.mult, op1=ALU.add,
                        accum_out=s_epst[:, 2 * tl:2 * tl + 1])
                vector.drain().then_inc(v12, 1)
                vector.wait_ge(a10, 3 * c + 1)
                norm_finish(0, 0, a12, 6 * c + 1)
                norm_finish(2, 3, a12, 6 * c + 2)
                for tl in range(2):
                    vector.tensor_scalar(out=s_u, in0=s_yep[:, D * tl:D * (tl + 1)],
                                         scalar1=s_epf[:, 3 * tl + 1:3 * tl + 2],
                                         scalar2=s_epf[:, 3 * tl + 2:3 * tl + 3],
                                         op0=ALU.mult, op1=ALU.subtract)
                    vector.drain()
                    vector.wait_ge(m_xe, 32 * c + 16 * (tl + 1))
                    vector.tensor_tensor(out=s_xp[:, D * tl:D * (tl + 1)],
                                         in0=s_xep, in1=s_u, op=ALU.add)
                    vector.drain().then_inc(ve1, 1)
                    vector.tensor_scalar(
                        out=junkv[:, 0:D], in0=s_xp[:, D * tl:D * (tl + 1)],
                        scalar1=1.0, scalar2=0.0, op0=ALU.mult, op1=ALU.add,
                        accum_out=s_epst[:, 4 + 2 * tl:5 + 2 * tl])
                vector.drain()
                vector.wait_ge(a10, 3 * c + 2)
                norm_finish(4, 6, a12, 6 * c + 3)
                norm_finish(6, 9, a12, 6 * c + 4)
                for tl in range(2):
                    vector.tensor_scalar(out=s_z[:],
                                         in0=s_xp[:, D * tl:D * (tl + 1)],
                                         scalar1=s_epf[:, 6 + 3 * tl + 1:
                                                       6 + 3 * tl + 2],
                                         scalar2=s_epf[:, 6 + 3 * tl + 2:
                                                       6 + 3 * tl + 3],
                                         op0=ALU.mult, op1=ALU.subtract)
                    vector.drain().then_inc(ve2, 1)
                    for gg in range(4):
                        evac([(s_zfm[:].rearrange("p (k t2) -> p k t2", k=16)
                               [:, 4 * gg:4 * (gg + 1), 128 * tl:128 * (tl + 1)],
                               512)])
                vector.sem_inc(v7, 1)
                if LIM < 5:
                    continue
                g = vq_n[0]
                vector.wait_ge(tq, g + 1)
                vector.tensor_copy(out=s_atm[:], in_=pts[g % 2][:, 0:16])
                vector.drain().then_inc(vq, 1)
                vq_n[0] += 1
                vector.tensor_tensor(out=s_atm[:], in0=s_atm[:], in1=s_b2bc[:],
                                     op=ALU.add)
                vector.drain()
                for tl in range(2):
                    vector.reduce_max(out=s_asm[:, tl:tl + 1],
                                      in_=s_atm[:, KEXP * tl:KEXP * (tl + 1)],
                                      axis=mybir.AxisListType.X, negate=True)
                vector.drain().then_inc(v12, 1)
                for i in range(4):
                    vector.wait_ge(t6, 4 * c + i + 1)
                    vector.tensor_copy(out=s_vx[:, TSUB * i:TSUB * (i + 1)],
                                       in_=pfs[i % 2][:, 0:TSUB])
                    vector.drain().then_inc(v8, 1)
                vector.wait_ge(a10, 3 * c + 3)
                for tl in range(2):
                    vector.reduce_sum(out=s_asm[:, 2 + tl:3 + tl],
                                      in_=s_aexp[:, KEXP * tl:KEXP * (tl + 1)],
                                      axis=mybir.AxisListType.X)
                vector.drain()
                vector.reciprocal(out=s_asm[:, 4:6], in_=s_asm[:, 2:4])
                vector.drain()
                for tl in range(2):
                    vector.tensor_scalar(out=s_anrm[:, KEXP * tl:KEXP * (tl + 1)],
                                         in0=s_aexp[:, KEXP * tl:KEXP * (tl + 1)],
                                         scalar1=s_asm[:, 4 + tl:5 + tl],
                                         scalar2=None, op0=ALU.mult)
                vector.drain().then_inc(v12, 1)
                g = vq_n[0]
                vector.wait_ge(tq, g + 1)
                vector.tensor_copy(out=s_af[:], in_=pts[g % 2][0:KEXP, 0:TSUB])
                vector.drain().then_inc(vq, 1)
                vector.sem_inc(a13, 1)
                vq_n[0] += 1
                for i in range(4):
                    vector.wait_ge(t7, 4 * c + i + 1)
                    vector.tensor_tensor(out=s_va[:, TSUB * i:TSUB * (i + 1)],
                                         in0=s_vx[:, TSUB * i:TSUB * (i + 1)],
                                         in1=pf0[:, 256 * (i % 2):
                                                 256 * (i % 2) + TSUB],
                                         op=ALU.mult)
                    vector.drain().then_inc(v9, 1)
                for gg in range(9):
                    if gg < 8:
                        for dl in range(2):
                            dto = 2 * gg + dl
                            vector.wait_ge(t8, 16 * c + dto + 1)
                            if dto >= 4:
                                vector.wait_ge(tq, TQ_CH * c + 26
                                               + (dto - 4) // 2 + 1)
                            vector.tensor_copy(out=s_ffsb[:, 256 * (dto % 4):
                                                        256 * (dto % 4) + 256],
                                               in_=pfs[dto % 2][:, 0:TSUB])
                            vector.drain().then_inc(v10, 1)
                    if gg >= 1:
                        gp = gg - 1
                        evac([(s_fftm[:, 256 * gp:256 * (gp + 1)], 256),
                              (s_fftm[:, D + 256 * gp:D + 256 * (gp + 1)], 256)])
                for tl in range(2):
                    vector.tensor_scalar(
                        out=junkv[:, 0:D], in0=s_fftm[:, D * tl:D * (tl + 1)],
                        scalar1=1.0, scalar2=0.0, op0=ALU.mult, op1=ALU.add,
                        accum_out=s_epst[:, 8 + 2 * tl:9 + 2 * tl])
                vector.drain().then_inc(v12, 1)
                vector.wait_ge(a11, c + 1)
                norm_finish(8, 0, a12, 6 * c + 5)
                norm_finish(10, 3, a12, 6 * c + 6)
                for tl in range(2):
                    vector.tensor_scalar(out=s_u, in0=s_fftm[:, D * tl:D * (tl + 1)],
                                         scalar1=s_epf[:, 3 * tl + 1:3 * tl + 2],
                                         scalar2=s_epf[:, 3 * tl + 2:3 * tl + 3],
                                         op0=ALU.mult, op1=ALU.subtract)
                    vector.drain()
                    if tl == 1:
                        vector.wait_ge(m_out, 32 * c + 16)
                    vector.tensor_tensor(out=s_out,
                                         in0=s_xp[:, D * tl:D * (tl + 1)],
                                         in1=s_u, op=ALU.add)
                    vector.drain().then_inc(v11, 1)

        @blk.scalar
        def _(scalar):
            scalar.wait_ge(m_ld, 16 * NLOAD)
            scalar.activation(out=s_A[:], in_=s_Alog[:], func=AF.Exp)
            scalar.drain().then_inc(a14, 1)
            for c in range(NCH):
                scalar.wait_ge(m_x1, 16 * (c + 1))
                if c >= 1:
                    scalar.wait_ge(v1b, 7 * (c - 1) + 1)
                for i in range(16):
                    scalar.activation(out=junka[:, 0:256],
                                      in_=s_xpro[:, 256 * i:256 * (i + 1)],
                                      func=AF.Square,
                                      accum_out=s_stats[:, 2 * i + 1:2 * i + 2])
                scalar.drain().then_inc(a1, 1)
                scalar.wait_ge(v1b, 7 * c + 1)
                scalar.activation(out=s_statf[:, 16:32], in_=s_statf[:, 32:48],
                                  func=AF.Sqrt, bias=s_eps[:])
                scalar.drain().then_inc(a1b, 1)
                if LIM < 2 or SUB < 2:
                    continue
                scalar.wait_ge(v2, c + 1)
                for dt in range(2):
                    scalar.activation(out=s_xconv[:, CH * dt:CH * (dt + 1)],
                                      in_=s_xconv[:, CH * dt:CH * (dt + 1)],
                                      func=AF.Silu, bias=s_cb[:, dt:dt + 1])
                scalar.drain().then_inc(a2, 1)
                if SUB < 3:
                    continue
                for p in range(4):
                    scalar.wait_ge(t2, 4 * c + p + 1)
                    scalar.activation(out=s_proj16[:, 512 * p:512 * (p + 1)],
                                      in_=(pp0 if p % 2 == 0 else pd0)[:, :],
                                      func=AF.Copy)
                    scalar.activation(out=s_proj16b[:, 512 * p:512 * (p + 1)],
                                      in_=pd1[32 * (p % 2):32 * (p % 2) + 32, :],
                                      func=AF.Copy)
                    scalar.drain().then_inc(a7, 1)
                if SUB < 5:
                    continue
                for idx in range(8):
                    dt, cp = idx // 4, idx % 4
                    scalar.wait_ge(t3, 8 * c + idx + 1)
                    ps = pds[idx % 2]
                    dsl = s_delta[:, CH * dt + 512 * cp:CH * dt + 512 * (cp + 1)]
                    lsl = s_spln[:, CH * dt + 512 * cp:CH * dt + 512 * (cp + 1)]
                    tmp = junka[:, 512:1024]
                    scalar.activation(out=dsl, in_=ps[:, :], func=AF.Relu,
                                      bias=s_dtb[:, dt:dt + 1])
                    scalar.activation(out=tmp, in_=ps[:, :], func=AF.Abs,
                                      bias=s_dtb[:, dt:dt + 1])
                    scalar.drain()
                    scalar.activation(out=tmp, in_=tmp, func=AF.Exp, scale=-1.0)
                    scalar.drain()
                    scalar.activation(out=lsl, in_=tmp, func=AF.Ln, bias=1.0)
                    scalar.drain().then_inc(a3, 1)
                if LIM < 3:
                    continue
                scalar.wait_ge(vrc, 1)
                for n in range(N):
                    if n >= 2:
                        scalar.wait_ge(v4, 32 * c + 2 * n - 2)
                    scalar.dma_start(
                        out=s_Bbc[:, CH * (n % 2):CH * (n % 2 + 1)],
                        in_=ar2_out[c].ap()[128 + n:129 + n, :
                                            ].partition_broadcast(128),
                    ).then_inc(m_bBs[n % 2], 16)
                    if n >= 2:
                        scalar.wait_ge(v5, 16 * c + n - 1)
                    scalar.dma_start(
                        out=s_Cbc[:, CH * (n % 2):CH * (n % 2 + 1)],
                        in_=ar2_out[c].ap()[144 + n:145 + n, :
                                            ].partition_broadcast(128),
                    ).then_inc(m_bCs[n % 2], 16)
                    for dt in range(2):
                        s = 2 * n + dt
                        if s == 0:
                            scalar.wait_ge(v3, c + 1)
                        if s >= 2:
                            scalar.wait_ge(v4, 32 * c + s - 1)
                        scalar.activation(out=s_a[:, CH * (s % 2):CH * (s % 2 + 1)],
                                          in_=s_delta[:, CH * dt:CH * (dt + 1)],
                                          func=AF.Exp,
                                          scale=s_A[:, N * dt + n:N * dt + n + 1])
                        scalar.drain().then_inc(a4, 1)
                    scalar.wait_ge(v4, 32 * c + 2 * n + 2)
                    scalar.activation(
                        out=s_carry[:].rearrange("p (dt n2) -> p dt n2", dt=2)
                            [:, :, n:n + 1].rearrange("p dt n2 -> p (dt n2)"),
                        in_=s_h[:].rearrange("p (sl dt t2) -> p sl dt t2",
                                             sl=2, dt=2)
                            [:, n % 2, :, CH - 1:CH].rearrange("p dt t2 -> p (dt t2)"),
                        func=AF.Copy)
                    scalar.drain().then_inc(a5, 1)
                if LIM < 4:
                    continue
                scalar.wait_ge(v12, 4 * c + 1)
                for tl in range(2):
                    scalar.activation(out=junka[:, 0:D],
                                      in_=s_yep[:, D * tl:D * (tl + 1)],
                                      func=AF.Square,
                                      accum_out=s_epst[:, 2 * tl + 1:2 * tl + 2])
                scalar.drain().then_inc(a10, 1)
                scalar.wait_ge(v1b, 7 * c + 2)
                scalar.activation(out=s_epf[:, 1:2], in_=s_epf[:, 1:2],
                                  func=AF.Sqrt, bias=s_eps[:])
                scalar.drain().then_inc(a12, 1)
                scalar.wait_ge(v1b, 7 * c + 3)
                scalar.activation(out=s_epf[:, 4:5], in_=s_epf[:, 4:5],
                                  func=AF.Sqrt, bias=s_eps[:])
                scalar.drain().then_inc(a12, 1)
                for tl in range(2):
                    scalar.wait_ge(ve1, 2 * c + tl + 1)
                    scalar.activation(out=junka[:, 0:D],
                                      in_=s_xp[:, D * tl:D * (tl + 1)],
                                      func=AF.Square,
                                      accum_out=s_epst[:, 5 + 2 * tl:6 + 2 * tl])
                scalar.drain().then_inc(a10, 1)
                scalar.wait_ge(v1b, 7 * c + 4)
                scalar.activation(out=s_epf[:, 7:8], in_=s_epf[:, 7:8],
                                  func=AF.Sqrt, bias=s_eps[:])
                scalar.drain().then_inc(a12, 1)
                scalar.wait_ge(v1b, 7 * c + 5)
                scalar.activation(out=s_epf[:, 10:11], in_=s_epf[:, 10:11],
                                  func=AF.Sqrt, bias=s_eps[:])
                scalar.drain().then_inc(a12, 1)
                if LIM < 5:
                    continue
                for e in range(16):
                    scalar.wait_ge(t4, 16 * c + e + 1)
                    scalar.activation(out=s_he[:], in_=pf0[:, 0:TSUB], func=AF.Relu,
                                      bias=s_b1[:, e:e + 1])
                    scalar.drain().then_inc(a6, 1)
                scalar.wait_ge(t5, c + 1)
                scalar.activation(out=s_alpha[:], in_=pf1[0:KEXP, 0:TSUB],
                                  func=AF.Copy)
                scalar.drain().then_inc(a8, 1)
                scalar.wait_ge(v12, 4 * c + 2)
                for tl in range(2):
                    scalar.activation(out=s_aexp[:, KEXP * tl:KEXP * (tl + 1)],
                                      in_=s_atm[:, KEXP * tl:KEXP * (tl + 1)],
                                      func=AF.Exp, bias=s_asm[:, tl:tl + 1])
                scalar.drain().then_inc(a10, 1)
                scalar.wait_ge(v12, 4 * c + 4)
                for tl in range(2):
                    scalar.activation(out=junka[:, 0:D],
                                      in_=s_fftm[:, D * tl:D * (tl + 1)],
                                      func=AF.Square,
                                      accum_out=s_epst[:, 9 + 2 * tl:10 + 2 * tl])
                scalar.drain().then_inc(a11, 1)
                scalar.wait_ge(v1b, 7 * c + 6)
                scalar.activation(out=s_epf[:, 1:2], in_=s_epf[:, 1:2],
                                  func=AF.Sqrt, bias=s_eps[:])
                scalar.drain().then_inc(a12, 1)
                scalar.wait_ge(v1b, 7 * c + 7)
                scalar.activation(out=s_epf[:, 4:5], in_=s_epf[:, 4:5],
                                  func=AF.Sqrt, bias=s_eps[:])
                scalar.drain().then_inc(a12, 1)

    return nc


def kernel(**inputs):
    x = np.ascontiguousarray(np.asarray(inputs["L0_attn_x"], np.float32)[0])
    T = x.shape[0]
    conv_w = np.asarray(inputs["L0_attn_conv_w"], np.float32)
    conv_b = np.asarray(inputs["L0_attn_conv_b"], np.float32)
    xproj_w = np.asarray(inputs["L0_attn_xproj_w"], np.float32)
    dtproj_w = np.asarray(inputs["L0_attn_dtproj_w"], np.float32)
    dtproj_b = np.asarray(inputs["L0_attn_dtproj_b"], np.float32)
    A_log = np.asarray(inputs["L0_attn_A_log"], np.float32)
    D_param = np.asarray(inputs["L0_attn_D_param"], np.float32)
    U = np.asarray(inputs["L0_ffn_U"], np.float32)
    V = np.asarray(inputs["L0_ffn_V"], np.float32)
    W1 = np.asarray(inputs["L0_ffn_W1"], np.float32)
    b1 = np.asarray(inputs["L0_ffn_b1"], np.float32)
    W2 = np.asarray(inputs["L0_ffn_W2"], np.float32)
    b2 = np.asarray(inputs["L0_ffn_b2"], np.float32)

    NCH = T // CH
    nc = build_nc(T)

    ident16 = np.eye(128, dtype=np.float16)
    sel16 = np.repeat(np.eye(KEXP, dtype=np.float16), 64, axis=1)
    zero32 = np.zeros((128, 32), np.float32)
    zero16 = np.zeros((128, 16), np.float16)
    VT = np.ascontiguousarray(V.transpose(2, 0, 1).reshape(D, 512))
    UT = np.ascontiguousarray(U.transpose(0, 2, 1).reshape(512, D))

    in_maps = []
    for k in range(NC):
        dsl = slice(DC * k, DC * (k + 1))
        x_trow = np.ascontiguousarray(np.concatenate(
            [x[CH * c + (CH // NC) * k: CH * c + (CH // NC) * (k + 1)]
             for c in range(NCH)], axis=0))
        in_maps.append({
            "x_dcol": np.ascontiguousarray(x[:, dsl]),
            "x_trow": x_trow,
            "w_cw": np.ascontiguousarray(conv_w[dsl, 0, :]),
            "w_cb": np.ascontiguousarray(conv_b[dsl, None]),
            "w_dtb": np.ascontiguousarray(dtproj_b[dsl, None]),
            "w_alog": np.ascontiguousarray(A_log[dsl]),
            "w_dp": np.ascontiguousarray(D_param[dsl, None]),
            "w_dtprojT": np.ascontiguousarray(dtproj_w[dsl].T),
            "w_xprojT": np.ascontiguousarray(xproj_w[:, dsl].T),
            "w_w1": W1,
            "w_b1": np.ascontiguousarray(b1[:, None]),
            "w_w2": W2,
            "w_b2": np.ascontiguousarray(b2[:, None]),
            "w_vt": VT,
            "w_ut": UT,
            "c_ident16": ident16,
            "c_sel16": sel16,
            "c_zero32": zero32,
            "c_eps": np.full((128, 1), EPS, np.float32),
            "c_b2bc": np.tile(b2[None, :], (128, 2)).astype(np.float32),
            "c_zero16": zero16,
        })
    res = run_bass_kernel_spmd(nc, in_maps, core_ids=list(range(NC)))
    out = np.empty((T, D), np.float32)
    for k in range(NC):
        oy = res.results[k]["o_y"]
        for c in range(NCH):
            out[CH * c + (CH // NC) * k: CH * c + (CH // NC) * (k + 1)] = \
                oy[(CH // NC) * c: (CH // NC) * (c + 1)]
    return out[None, :, :]
